# version 10
# speedup vs baseline: 1.0402x; 1.0035x over previous
import numpy as np
import ml_dtypes
from contextlib import ExitStack
import concourse.bass as bass
import concourse.mybir as mybir
from concourse.bass_utils import run_bass_kernel_spmd

F32 = mybir.dt.float32
BF16 = mybir.dt.bfloat16
FP16 = mybir.dt.float16
AF = mybir.ActivationFunctionType
OP = mybir.AluOpType
AX = mybir.AxisListType
bf16 = ml_dtypes.bfloat16

S, D, DIN, NN = 2048, 1024, 2048, 16
ST, DC, KD = 16, 16, 8
TC, TW = 4, 512
HH, PP, EE, FF = 16, 64, 8, 4096
EPS = 1e-5

_TAPS = ()
_STOP = "FULL"
TAP_RESULTS = {}


def _host_prep(inp):
    f = np.float32
    aux = {}
    wi = np.asarray(inp["L0_attn_in_proj_w"], f)
    t = wi.reshape(32, 128, 8, 128).transpose(0, 3, 2, 1)
    t = np.ascontiguousarray(t.reshape(32, 128, 1024))
    aux["winp_x"] = t[:16].copy()
    aux["winp_z"] = t[16:].copy()

    cw = np.asarray(inp["L0_attn_conv_w"], f).reshape(DIN, 4)
    aux["convw"] = np.ascontiguousarray(
        cw.reshape(DC, 128, 4).transpose(1, 0, 2).reshape(128, DC * 4))
    for nm, key in [("convb", "L0_attn_conv_b"), ("dskip", "L0_attn_D"),
                    ("dtw", "L0_attn_dt_proj_w"), ("dtb", "L0_attn_dt_proj_b")]:
        v = np.asarray(inp[key], f).reshape(DIN)
        aux[nm] = np.ascontiguousarray(v.reshape(DC, 128).T)
    alog = np.asarray(inp["L0_attn_A_log"], f)
    aux["alog"] = np.ascontiguousarray(
        alog.reshape(DC, 128, NN).transpose(1, 0, 2).reshape(128, DC * NN))

    xp = np.asarray(inp["L0_attn_x_proj_w"], f)
    pad = np.zeros((DIN, 65), f)
    pad[:, 0:16] = xp[1:17].T
    pad[:, 32:48] = xp[17:33].T
    pad[:, 64] = xp[0]
    aux["xprojT"] = np.ascontiguousarray(
        pad.reshape(16, 128, 65).transpose(1, 0, 2).reshape(128, 16 * 65))

    wo = np.asarray(inp["L0_attn_out_proj_w"], f)
    aux["woutT"] = np.ascontiguousarray(wo.T.reshape(16, 128, 1024))

    wq = np.asarray(inp["L0_ffn_Wq"], f)
    wk = np.asarray(inp["L0_ffn_Wk"], f)
    qx1 = wq[:, :, 0:16].transpose(1, 0, 2).reshape(D, 256)
    qx2 = wq[:, :, 16:32].transpose(1, 0, 2).reshape(D, 256)
    kx1 = wk[:, :, 0:16].transpose(1, 0, 2).reshape(D, 256)
    kx2 = wk[:, :, 16:32].transpose(1, 0, 2).reshape(D, 256)
    qp = wq[:, :, 32:64].transpose(1, 0, 2).reshape(D, 512)
    kp = wk[:, :, 32:64].transpose(1, 0, 2).reshape(D, 512)
    wqk = np.concatenate([qx1, qx2, kx1, kx2, qp, kp], axis=1)
    t = wqk.reshape(8, 128, 16, 128).transpose(2, 1, 0, 3)
    aux["wqk"] = np.ascontiguousarray(t.reshape(16, 128, 1024))

    wsel = np.concatenate([
        np.asarray(inp["L0_ffn_Wsel_v"], f).reshape(128, D),
        np.asarray(inp["L0_ffn_Wsel_o"], f).reshape(128, D)], axis=0)
    t = wsel.T.reshape(8, 128, 2, 128).transpose(2, 1, 0, 3)
    aux["wsel"] = np.ascontiguousarray(t.reshape(2, 128, 1024))

    wv = np.asarray(inp["L0_ffn_Wv"], f)
    t = wv.transpose(0, 2, 3, 1).reshape(HH, KD, 128, PP * EE)
    aux["wv"] = np.ascontiguousarray(t)

    wo2 = np.asarray(inp["L0_ffn_Wo"], f)
    aux["wo"] = np.ascontiguousarray(
        wo2.reshape(HH, 4, 2 * PP, D).reshape(64, 128, D))

    w1 = np.asarray(inp["L0_ffn_W1"], f)
    t = w1.reshape(8, 128, 32, 128).transpose(2, 1, 0, 3)
    aux["w1"] = np.ascontiguousarray(t.reshape(32, 128, 1024))
    aux["w2"] = np.ascontiguousarray(
        np.asarray(inp["L0_ffn_W2"], f).reshape(32, 128, 1024))

    aux["b1"] = np.ascontiguousarray(np.asarray(inp["L0_ffn_b1"], f).reshape(32, 128).T)
    aux["b2row"] = np.ascontiguousarray(np.asarray(inp["L0_ffn_b2"], f).reshape(8, 128).T)
    aux["b2r"] = np.ascontiguousarray(np.asarray(inp["L0_ffn_b2"], f).reshape(1, D))
    for nm, key in [("lng1", "L0_ffn_ln1_g"), ("lnb1", "L0_ffn_ln1_b"),
                    ("lng2", "L0_ffn_ln2_g"), ("lnb2", "L0_ffn_ln2_b")]:
        aux[nm] = np.ascontiguousarray(np.asarray(inp[key], f).reshape(1, D))

    half = 16
    inv = (10000.0 ** (-np.arange(half, dtype=f) / half))
    ang = np.arange(S, dtype=f)[:, None] * inv[None, :]
    aux["cosr"] = np.ascontiguousarray(np.tile(np.cos(ang).T.astype(f), (8, 1)))
    aux["sinr"] = np.ascontiguousarray(np.tile(np.sin(ang).T.astype(f), (8, 1)))
    return aux


def _build(taps=(), stop="FULL"):
    nc = bass.Bass("TRN2", target_bir_lowering=False)

    def din(name, shape, dt=F32):
        return nc.dram_tensor(name, list(shape), dt, kind="ExternalInput").ap()

    d_x = din("xin", [S, D])
    d_winpx = din("winp_x", [16, 128, 1024])
    d_winpz = din("winp_z", [16, 128, 1024])
    d_convw = din("convw", [128, DC * 4])
    d_convb = din("convb", [128, DC])
    d_dskip = din("dskip", [128, DC])
    d_dtw = din("dtw", [128, DC])
    d_dtb = din("dtb", [128, DC])
    d_alog = din("alog", [128, DC * NN])
    d_xprojT = din("xprojT", [128, 16 * 65])
    d_woutT = din("woutT", [16, 128, 1024])
    d_wqk = din("wqk", [16, 128, 1024])
    d_wsel = din("wsel", [2, 128, 1024])
    d_wv = din("wv", [16, 8, 128, 512])
    d_wo = din("wo", [64, 128, 1024])
    d_w1 = din("w1", [32, 128, 1024])
    d_w2 = din("w2", [32, 128, 1024])
    d_b1 = din("b1", [128, 32])
    d_b2row = din("b2row", [128, 8])
    d_b2r = din("b2r", [1, D])
    d_lng1 = din("lng1", [1, D])
    d_lnb1 = din("lnb1", [1, D])
    d_lng2 = din("lng2", [1, D])
    d_lnb2 = din("lnb2", [1, D])
    d_cosr = din("cosr", [128, S])
    d_sinr = din("sinr", [128, S])

    d_out = nc.dram_tensor("out", [S, D], F32, kind="ExternalOutput").ap()
    d_x1sp = nc.dram_tensor("x1_sp", [S, D], BF16, kind="Internal").ap()
    d_xpd = nc.dram_tensor("xp_d", [65, S], BF16, kind="Internal").ap()
    d_asp = nc.dram_tensor("a_sp", [S, D], BF16, kind="Internal").ap()
    d_mhat = nc.dram_tensor("mhat_sp", [16, S], BF16, kind="Internal").ap()
    d_osp = nc.dram_tensor("o_sp", [16, 64, S], BF16, kind="Internal").ap()
    d_w1b = nc.dram_tensor("w1b", [32, 128, 1024], BF16, kind="Internal").ap()
    d_w2b = nc.dram_tensor("w2b", [32, 128, 1024], BF16, kind="Internal").ap()
    d_wob = nc.dram_tensor("wob", [64, 128, 1024], BF16, kind="Internal").ap()
    d_dbg = nc.dram_tensor("dbg", [128, DC * S], BF16, kind="ExternalOutput").ap()         if stop in ("A", "B", "C") else None
    d_srcsp = nc.dram_tensor("src_sp", [S, D], BF16, kind="Internal").ap()

    tap_handles = {}
    def tapt(name, shape, dt=F32):
        if name in taps:
            tap_handles[name] = nc.dram_tensor(
                "tap_" + name, list(shape), dt, kind="ExternalOutput").ap()
        return tap_handles.get(name)

    ident_bf = nc.inline_tensor(np.eye(128, dtype=bf16), name="identbf").ap()
    p2np = np.zeros((2, 128), dtype=bf16)
    p2np[0, :64] = 1
    p2np[1, 64:] = 1
    p2pat = nc.inline_tensor(p2np, name="p2pat").ap()
    p16np = np.zeros((16, 128), dtype=bf16)
    for q in range(128):
        p16np[q // 8, q] = 1
    p16pat = nc.inline_tensor(p16np, name="p16pat").ap()
    ones64 = nc.inline_tensor(np.ones((64, 1), dtype=bf16), name="ones64").ap()

    es = ExitStack()
    sbt = lambda name, shape, dt: es.enter_context(nc.sbuf_tensor(name, list(shape), dt))

    c_convw = sbt("c_convw", [128, DC * 4], F32)
    c_convb = sbt("c_convb", [128, DC], F32)
    c_dskip = sbt("c_dskip", [128, DC], F32)
    c_dtw = sbt("c_dtw", [128, DC], F32)
    c_dtb = sbt("c_dtb", [128, DC], F32)
    c_A = sbt("c_A", [128, DC * NN], F32)
    c_id = sbt("c_id", [128, 128], BF16)
    c_p2 = sbt("c_p2", [2, 128], BF16)
    c_p16 = sbt("c_p16", [16, 128], BF16)
    c_ones64 = sbt("c_ones64", [64, 1], BF16)
    c_b1 = sbt("c_b1", [128, 32], F32)
    c_b2 = sbt("c_b2", [128, 8], F32)
    c_eps = sbt("c_eps", [128, 1], F32)

    xc_a = sbt("xc_a", [128, DC * S], BF16)
    act_a = sbt("act_a", [128, KD * S], BF16)
    sjunk = sbt("sjunk", [128, D], F32)

    xc_tile = lambda dc: xc_a[:, dc * S:(dc + 1) * S]
    ln0T = lambda k: act_a[:, k * S:(k + 1) * S]

    semn, cnt = {}, {}
    def mksem(*names):
        for n in names:
            semn[n] = es.enter_context(nc.semaphore(n))
            cnt[n] = 0
    mksem("qc", "qga", "qcx", "ax0", "ax1", "bw0", "bw1", "gb",
          "wz0", "wz1", "wo0", "wo1", "wo2", "xps", "sx0", "sx1", "ssr", "xr",
          "t", "v", "s", "eps", "vf", "sf",
          "dts", "da", "scv", "zt", "szs", "gy", "ot", "epv", "ep2")

    def inc(inst, name, k=1):
        inst.then_inc(semn[name], k)
        cnt[name] += k
        return cnt[name]

    def wge(eng, name, val):
        if val > 0:
            eng.wait_ge(semn[name], val)

    def dma(eng, qname, out, in_):
        eng.dma_start(out=out, in_=in_).then_inc(semn[qname], 16)
        cnt[qname] += 16
        return cnt[qname]

    def repl_ap(src_row_ap, width, off_elems=0, nparts=128):
        t0 = src_row_ap
        return bass.AP(t0.tensor, t0.offset + off_elems,
                       [[t0.ap[0][0], 1], [0, nparts], [1, width]])

    block = es.enter_context(nc.Block())

    @block.sync
    def _(sync):
        for dst, src in [(c_convw, d_convw), (c_convb, d_convb), (c_dskip, d_dskip),
                         (c_dtw, d_dtw), (c_dtb, d_dtb),
                         (c_id, ident_bf), (c_p2, p2pat), (c_p16, p16pat),
                         (c_ones64, ones64), (c_b1, d_b1), (c_b2, d_b2row)]:
            dma(sync, "qc", dst[:], src)
    QS_CONST = cnt["qc"]

    @block.gpsimd
    def _(gp):
        dma(gp, "qga", c_A[:], d_alog)
    QG_ALOG = cnt["qga"]

    @block.scalar
    def _(s):
        wge(s, "qga", QG_ALOG)
        inc(s.activation(c_A[:], c_A[:], AF.Exp), "s")
    S_AEXP = cnt["s"]
    @block.vector
    def _(v):
        v.memset(c_eps[:], EPS)
        wge(v, "s", S_AEXP)
        inc(v.tensor_scalar(c_A[:], c_A[:], -1.0, None, op0=OP.mult), "v")
    V_ADONE = cnt["v"]

    esSA = ExitStack()
    sba = lambda name, shape, dt: esSA.enter_context(nc.sbuf_tensor(name, list(shape), dt))
    xga = sba("xga", [128, 4 * D], F32)
    xgb = sba("xgb", [128, 4 * D], F32)
    xng = sba("xng", [128, 4 * D], BF16)
    stA = {nm: sba("stA_" + nm, [128, 4], F32)
           for nm in ["sum", "nm", "m2", "ssq", "var", "lv", "r"]}
    psT = esSA.enter_context(nc.psum_tensor("psT", [128, 4096], BF16))

    A_V0, A_S0, A_T0, A_E0 = (cnt["v"], cnt["s"], cnt["t"], cnt["eps"])
    SSQ_TICK = [0] * 4

    @block.sync
    def _(sync):
        for g in range(4):
            buf = xga if g % 2 == 0 else xgb
            if g >= 2:
                wge(sync, "v", A_V0 + 3 * (g - 1))
            for i in range(4):
                sti = g * 4 + i
                dma(sync, "ax0" if g % 2 == 0 else "ax1", buf[:, i * D:(i + 1) * D],
                    d_x[sti * 128:(sti + 1) * 128, :])

    @block.vector
    def _(v):
        for g in range(4):
            buf = xga if g % 2 == 0 else xgb
            wge(v, "ax0" if g % 2 == 0 else "ax1", 16 * 4 * (g // 2 + 1))
            for i in range(4):
                last = v.tensor_reduce(stA["sum"][:, i:i + 1],
                                       buf[:, i * D:(i + 1) * D], axis=AX.X, op=OP.add)
            inc(last, "v")
            wge(v, "sf", SSQ_TICK[g])
            wge(v, "s", A_S0 + 2 * g + 1)
            for i in range(4):
                last = v.scalar_tensor_tensor(stA["var"][:, i:i + 1],
                                              stA["ssq"][:, i:i + 1], 1.0 / D,
                                              stA["m2"][:, i:i + 1],
                                              op0=OP.mult, op1=OP.subtract)
            inc(last, "v")
            wge(v, "s", A_S0 + 2 * g + 2)
            if g >= 1:
                wge(v, "t", A_T0 + g)
            for i in range(4):
                last = v.tensor_scalar(xng[:, i * D:(i + 1) * D],
                                       buf[:, i * D:(i + 1) * D],
                                       stA["nm"][:, i:i + 1], stA["r"][:, i:i + 1],
                                       op0=OP.add, op1=OP.mult)
            inc(last, "v")

    @block.scalar
    def _(s):
        def evacA(g):
            wge(s, "t", A_T0 + g + 1)
            for j in range(8):
                last = s.copy(ln0T(j)[:, g * 512:(g + 1) * 512],
                              psT[:, j * 512:(j + 1) * 512])
            inc(last, "eps")
        for g in range(4):
            buf = xga if g % 2 == 0 else xgb
            wge(s, "ax0" if g % 2 == 0 else "ax1", 16 * 4 * (g // 2 + 1))
            for i in range(4):
                if i > 0:
                    wge(s, "sf", cnt["sf"])
                last = s.activation(sjunk[:], buf[:, i * D:(i + 1) * D], AF.Square,
                                    accum_out=stA["ssq"][:, i:i + 1])
                inc(last, "sf")
            SSQ_TICK[g] = cnt["sf"]
            wge(s, "v", A_V0 + 3 * g + 1)
            for i in range(4):
                s.activation(stA["m2"][:, i:i + 1], stA["sum"][:, i:i + 1],
                             AF.Square, scale=1.0 / D)
                last = s.activation(stA["nm"][:, i:i + 1], stA["sum"][:, i:i + 1],
                                    AF.Copy, scale=-1.0 / D)
            inc(last, "s")
            wge(s, "v", A_V0 + 3 * g + 2)
            for i in range(4):
                last = s.activation(stA["lv"][:, i:i + 1], stA["var"][:, i:i + 1],
                                    AF.Ln, bias=c_eps[:])
            wge(s, "sf", inc(last, "sf"))
            for i in range(4):
                last = s.activation(stA["r"][:, i:i + 1], stA["lv"][:, i:i + 1],
                                    AF.Exp, scale=-0.5)
            inc(last, "s")
            if g >= 1:
                evacA(g - 1)
        evacA(3)
    A_EVAC_END = cnt["eps"]

    @block.tensor
    def _(t):
        wge(t, "qc", QS_CONST)
        for g in range(4):
            wge(t, "v", A_V0 + 3 * g + 3)
            if g >= 1:
                wge(t, "eps", A_E0 + g)
            for j in range(8):
                for i in range(4):
                    last = t.transpose(psT[:, (j * 4 + i) * 128:(j * 4 + i + 1) * 128],
                                       xng[:, i * D + j * 128: i * D + (j + 1) * 128],
                                       c_id[:])
            inc(last, "t")

    if stop == "A":
        @block.sync
        def _(sync):
            wge(sync, "eps", A_EVAC_END)
            for k in range(KD):
                dma(sync, "xr", d_dbg[:, k * S:(k + 1) * S], ln0T(k))
            wge(sync, "xr", cnt["xr"])
        esSA.close()
        es.close()
        return nc, tap_handles

    esSA.close()
    esA = ExitStack()
    psA = esA.enter_context(nc.psum_tensor("psA", [128, 2048], F32))
    psB = esA.enter_context(nc.psum_tensor("psB", [128, 2048], F32))

    esSB = ExitStack()
    sbb = lambda name, shape, dt: esSB.enter_context(nc.sbuf_tensor(name, list(shape), dt))
    wbufB = [sbb(f"wbufB{i}", [128, 1024], BF16) for i in range(2)]
    xsB = [sbb(f"xsB{i}", [128, 2051], F32) for i in range(2)]
    caccB = [sbb(f"caccB{i}", [128, 2048], F32) for i in range(2)]
    B_T0, B_S0, B_V0, B_E0 = (cnt["t"], cnt["s"], cnt["v"], cnt["eps"])

    @block.vector
    def _(v):
        v.memset(xsB[0][:, 0:3], 0.0)
        inc(v.memset(xsB[1][:, 0:3], 0.0), "v")
    B_VMEM = cnt["v"]

    @block.gpsimd
    def _(gp):
        wge(gp, "eps", A_EVAC_END)
        for m in range(16):
            if m >= 2:
                wge(gp, "t", B_T0 + m - 1)
            dma(gp, "bw0" if m % 2 == 0 else "bw1", wbufB[m % 2][:], d_winpx[m])

    @block.tensor
    def _(t):
        for m in range(16):
            wge(t, "bw0" if m % 2 == 0 else "bw1", 16 * (m // 2 + 1))
            if m == 0:
                wge(t, "eps", A_EVAC_END)
            if m >= 2:
                wge(t, "eps", B_E0 + (m - 2) + 1)
            ps = psA if m % 2 == 0 else psB
            for n in range(4):
                for k in range(8):
                    last = t.matmul(ps[:, n * 512:(n + 1) * 512],
                                    wbufB[m % 2][:, k * 128:(k + 1) * 128],
                                    ln0T(k)[:, n * 512:(n + 1) * 512],
                                    start=(k == 0), stop=(k == 7))
            inc(last, "t")

    @block.scalar
    def _(s):
        def siluB(m):
            wge(s, "v", B_VMEM + m + 1)
            inc(s.activation(xc_tile(m), caccB[m % 2][:], AF.Silu), "s")
        for m in range(16):
            wge(s, "t", B_T0 + m + 1)
            if m == 0:
                wge(s, "v", B_VMEM)
            if m >= 2:
                wge(s, "v", B_VMEM + (m - 2) + 1)
            ps = psA if m % 2 == 0 else psB
            for n in range(4):
                last = s.copy(xsB[m % 2][:, 3 + n * 512: 3 + (n + 1) * 512],
                              ps[:, n * 512:(n + 1) * 512])
            inc(last, "eps")
            if m >= 1:
                siluB(m - 1)
        siluB(15)
    B_SILU_END = cnt["s"]

    @block.vector
    def _(v):
        wge(v, "qc", QS_CONST)
        for m in range(16):
            wge(v, "eps", B_E0 + m + 1)
            if m >= 2:
                wge(v, "s", B_S0 + (m - 2) + 1)
            x0, ca = xsB[m % 2], caccB[m % 2]
            f = inc(v.tensor_scalar(ca[:], x0[:, 0:2048],
                                    c_convw[:, 4 * m: 4 * m + 1],
                                    c_convb[:, m:m + 1], op0=OP.mult, op1=OP.add), "vf")
            for kk in range(1, 4):
                wge(v, "vf", f)
                last = v.scalar_tensor_tensor(ca[:], x0[:, kk:kk + 2048],
                                              c_convw[:, 4 * m + kk: 4 * m + kk + 1],
                                              ca[:], op0=OP.mult, op1=OP.add)
                if kk < 3:
                    f = inc(last, "vf")
            inc(last, "v")

    xprojT_sb = sbb("xprojT_sb", [128, 16 * 65], BF16)
    xpstg = [sbb(f"xpstg{i}", [128, 512], BF16) for i in range(2)]
    C_T0, C_E0 = cnt["t"], cnt["eps"]

    @block.gpsimd
    def _(gp):
        dma(gp, "qcx", xprojT_sb[:], d_xprojT)

    @block.tensor
    def _(t):
        wge(t, "qcx", 16)
        wge(t, "s", B_SILU_END)
        for n in range(4):
            ps = psA if n % 2 == 0 else psB
            if n == 0:
                wge(t, "eps", B_E0 + 16)
            if n >= 2:
                wge(t, "eps", C_E0 + (n - 2) + 1)
            for k in range(16):
                last = t.matmul(ps[0:65, 0:512],
                                xprojT_sb[:, k * 65:(k + 1) * 65],
                                xc_tile(k)[:, n * 512:(n + 1) * 512],
                                start=(k == 0), stop=(k == 15))
            inc(last, "t")

    @block.scalar
    def _(s):
        for n in range(4):
            wge(s, "t", C_T0 + n + 1)
            if n >= 2:
                wge(s, "xps", 16 * n)
            ps = psA if n % 2 == 0 else psB
            inc(s.copy(xpstg[n % 2][0:65, :], ps[0:65, 0:512]), "eps")

    XPQ = [0] * 4
    @block.sync
    def _(sync):
        for n in range(4):
            wge(sync, "eps", C_E0 + n + 1)
            XPQ[n] = dma(sync, "xps", d_xpd[:, n * 512:(n + 1) * 512],
                         xpstg[n % 2][0:65, :])
    C_XP_DONE = XPQ[3]
    if stop in ("B", "C"):
        @block.sync
        def _(sync):
            wge(sync, "xps", C_XP_DONE)
            for dc in range(DC):
                dma(sync, "xr", d_dbg[:, dc * S:(dc + 1) * S], xc_tile(dc))
            wge(sync, "xr", cnt["xr"])
        esSB.close()
        esA.close()
        es.close()
        return nc, tap_handles
    esSB.close()

    esSD = ExitStack()
    sbd = lambda name, shape, dt: esSD.enter_context(nc.sbuf_tensor(name, list(shape), dt))
    dtr_b = sbd("dtr_b", [128, TW], F32)
    bb_sb = sbd("bb_sb", [128, NN * TW], BF16)
    cb_sb = sbd("cb_sb", [128, NN * TW], BF16)
    dtD = [sbd(f"dtD{i}", [128, TW], F32) for i in range(2)]
    daD = [sbd(f"daD{i}", [128, TW], F32) for i in range(2)]
    dxD = [sbd(f"dxD{i}", [128, TW], BF16) for i in range(2)]
    uD = sbd("uD", [128, TW], BF16)
    h_band = sbd("h_band", [128, NN * TW], FP16)
    y_sl = sbd("y_sl", [128, TW], F32)
    szall = sbd("szall", [128, DC * TW], BF16)
    hcol = sbd("hcol", [128, DC * NN], F32)
    wzD = [sbd(f"wzD{i}", [128, 1024], BF16) for i in range(2)]
    woD = [sbd(f"woD{i}", [128, 1024], BF16) for i in range(3)]
    xreD = sbd("xreD", [128, D], F32)
    x1tD = [sbd(f"x1tD{i}", [128, D], BF16) for i in range(2)]
    srcbD = sbd("srcbD", [128, D], BF16)
    epD = {nm: sbd("epD_" + nm, [128, 2], F32)
           for nm in ["sum", "nm", "m2", "ssq", "var", "lv", "r",
                      "sum2", "nm2", "m22", "ssq2", "var2", "lv2", "r2"]}

    zti = lambda tc, dc: tc * DC + dc + 1
    gni = lambda tc, dc, n: (tc * DC + dc) * NN + n + 1
    epidx = lambda tc, q: tc * 4 + q

    QG_TCB = [0] * TC
    QG_WZ = [[0] * DC for _ in range(TC)]
    QG_WO = [[0] * 16 for _ in range(TC)]

    @block.vector
    def _(v):
        wge(v, "eps", C_E0 + 4)
        inc(v.memset(hcol[:], 0.0), "vf")
    D_VF_MEM = cnt["vf"]

    @block.gpsimd
    def _(gp):
        wge(gp, "xps", C_XP_DONE)
        for tc in range(TC):
            if tc >= 1:
                wge(gp, "scv", tc * DC * NN)
            dma(gp, "gb", dtr_b[:], repl_ap(d_xpd[64:65, :], TW, off_elems=tc * TW))
            for n in range(NN):
                dma(gp, "gb", bb_sb[:, n * TW:(n + 1) * TW],
                    repl_ap(d_xpd[n:n + 1, :], TW, off_elems=tc * TW))
                dma(gp, "gb", cb_sb[:, n * TW:(n + 1) * TW],
                    repl_ap(d_xpd[32 + n:33 + n, :], TW, off_elems=tc * TW))
            QG_TCB[tc] = cnt["gb"]
            for dc in range(DC):
                gidx = tc * DC + dc
                if gidx >= 2:
                    wge(gp, "zt", gidx - 1)
                QG_WZ[tc][dc] = dma(gp, "wz0" if dc % 2 == 0 else "wz1",
                                    wzD[dc % 2][:], d_winpz[dc])
            for k in range(16):
                gk = tc * 16 + k
                if gk >= 3:
                    wge(gp, "ot", gk - 2)
                QG_WO[tc][k] = dma(gp, ["wo0", "wo1", "wo2"][k % 3],
                                   woD[k % 3][:], d_woutT[k])

    @block.tensor
    def _(t):
        for tc in range(TC):
            for dc in range(DC):
                gidx = tc * DC + dc
                wge(t, "wz0" if dc % 2 == 0 else "wz1", QG_WZ[tc][dc])
                if tc == 0 and dc == 0:
                    wge(t, "eps", C_E0 + 4)
                if tc >= 1 and dc == 0:
                    wge(t, "epv", 6 * epidx(tc - 1, 0) + 3)
                if gidx >= 2:
                    wge(t, "szs", gidx - 1)
                ps = psA[:, (dc % 2) * 512:(dc % 2) * 512 + 512]
                for k in range(8):
                    last = t.matmul(ps, wzD[dc % 2][:, k * 128:(k + 1) * 128],
                                    ln0T(k)[:, tc * TW:(tc + 1) * TW],
                                    start=(k == 0), stop=(k == 7))
                inc(last, "zt")
            wge(t, "gy", DC * (tc + 1))
            if tc >= 1:
                for q in range(1, 4):
                    wge(t, "epv", 6 * epidx(tc - 1, q) + 3)
            for k in range(16):
                wge(t, ["wo0", "wo1", "wo2"][k % 3], QG_WO[tc][k])
                for q in range(4):
                    ps = psA if q < 2 else psB
                    off = (q % 2) * 1024
                    lhs = xc_tile(k)[:, tc * TW + q * 128: tc * TW + (q + 1) * 128]
                    for hf in range(2):
                        last = t.matmul(ps[:, off + hf * 512: off + (hf + 1) * 512],
                                        lhs, woD[k % 3][:, hf * 512:(hf + 1) * 512],
                                        start=(k == 0), stop=(k == 15))
                inc(last, "ot")

    @block.scalar
    def _(s):
        wge(s, "qc", QS_CONST)
        wge(s, "v", V_ADONE)
        for tc in range(TC):
            for dc in range(DC):
                gidx = tc * DC + dc
                wge(s, "zt", zti(tc, dc))
                if tc > 0 and dc == 0:
                    wge(s, "gy", DC * tc)
                ps = psA[:, (dc % 2) * 512:(dc % 2) * 512 + 512]
                inc(s.activation(szall[:, dc * TW:(dc + 1) * TW], ps, AF.Silu), "szs")
            for dc in range(DC):
                gidx = tc * DC + dc
                if dc == 0:
                    wge(s, "gb", QG_TCB[tc])
                dt = dtD[dc % 2]
                if gidx >= 2:
                    wge(s, "scv", (gidx - 1) * NN)
                last = s.activation(dt[:], dtr_b[:], AF.Exp,
                                    scale=c_dtw[:, dc:dc + 1], bias=c_dtb[:, dc:dc + 1])
                wge(s, "sf", inc(last, "sf"))
                last = s.activation(dt[:], dt[:], AF.Ln, bias=1.0)
                wge(s, "dts", inc(last, "dts"))
                for n in range(NN):
                    gn = gni(tc, dc, n)
                    if gn >= 3:
                        wge(s, "scv", gn - 2)
                    inc(s.activation(daD[n % 2][:], dt[:], AF.Exp,
                                     scale=c_A[:, dc * NN + n: dc * NN + n + 1]), "da")
            for q in range(4):
                i = epidx(tc, q)
                b = q % 2
                ps = psA if q < 2 else psB
                pso = ps[:, (q % 2) * 1024:(q % 2) * 1024 + 1024]
                wge(s, "epv", 6 * i + 1)
                inc(s.activation(sjunk[:], pso, AF.Square,
                                 accum_out=epD["ssq"][:, b:b + 1]), "ep2")
                s.activation(epD["m2"][:, b:b + 1], epD["sum"][:, b:b + 1],
                             AF.Square, scale=1.0 / D)
                inc(s.activation(epD["nm"][:, b:b + 1], epD["sum"][:, b:b + 1],
                                 AF.Copy, scale=-1.0 / D), "ep2")
                wge(s, "epv", 6 * i + 2)
                last = s.activation(epD["lv"][:, b:b + 1], epD["var"][:, b:b + 1],
                                    AF.Ln, bias=c_eps[:])
                wge(s, "sf", inc(last, "sf"))
                inc(s.activation(epD["r"][:, b:b + 1], epD["lv"][:, b:b + 1],
                                 AF.Exp, scale=-0.5), "ep2")
                wge(s, "epv", 6 * i + 3)
                inc(s.activation(sjunk[:], x1tD[b][:], AF.Square,
                                 accum_out=epD["ssq2"][:, b:b + 1]), "ep2")
                wge(s, "epv", 6 * i + 4)
                s.activation(epD["m22"][:, b:b + 1], epD["sum2"][:, b:b + 1],
                             AF.Square, scale=1.0 / D)
                inc(s.activation(epD["nm2"][:, b:b + 1], epD["sum2"][:, b:b + 1],
                                 AF.Copy, scale=-1.0 / D), "ep2")
                wge(s, "epv", 6 * i + 5)
                last = s.activation(epD["lv2"][:, b:b + 1], epD["var2"][:, b:b + 1],
                                    AF.Ln, bias=c_eps[:])
                wge(s, "sf", inc(last, "sf"))
                inc(s.activation(epD["r2"][:, b:b + 1], epD["lv2"][:, b:b + 1],
                                 AF.Exp, scale=-0.5), "ep2")

    QS_RELOAD = [0] * (TC * 4)
    QS_X1SP = [0] * (TC * 4)
    QS_SRCSP = [0] * (TC * 4)
    @block.sync
    def _(sync):
        wge(sync, "xps", C_XP_DONE)
        QS_RELOAD[0] = dma(sync, "xr", xreD[:], d_x[0:128, :])
        for tc in range(TC):
            for q in range(4):
                i = epidx(tc, q)
                b = q % 2
                sx = "sx0" if b == 0 else "sx1"
                wge(sync, "epv", 6 * i + 3)
                dma(sync, sx, d_x1sp[i * 128:(i + 1) * 128, :], x1tD[b][:])
                if stop == "D":
                    dma(sync, sx, d_out[i * 128:(i + 1) * 128, :], x1tD[b][:])
                QS_X1SP[i] = cnt[sx]
                if i + 1 < TC * 4:
                    QS_RELOAD[i + 1] = dma(sync, "xr", xreD[:],
                                           d_x[(i + 1) * 128:(i + 2) * 128, :])
                wge(sync, "epv", 6 * i + 6)
                dma(sync, "ssr", d_srcsp[i * 128:(i + 1) * 128, :], srcbD[:])
                QS_SRCSP[i] = cnt["ssr"]

    @block.vector
    def _(v):
        wge(v, "qc", QS_CONST)
        for tc in range(TC):
            wge(v, "gb", QG_TCB[tc])
            if tc == 0:
                wge(v, "vf", D_VF_MEM)
            for dc in range(DC):
                gidx = tc * DC + dc
                wge(v, "dts", gidx + 1)
                if gidx >= 1:
                    wge(v, "gy", gidx)
                fu = inc(v.tensor_tensor(uD[:], dtD[dc % 2][:],
                                         xc_tile(dc)[:, tc * TW:(tc + 1) * TW],
                                         op=OP.mult), "vf")
                fdx = [0] * NN
                fscan = [0] * NN
                wge(v, "vf", fu)
                fdx[0] = inc(v.tensor_tensor(dxD[0][:], uD[:], bb_sb[:, 0:TW],
                                             op=OP.mult), "vf")
                for n in range(NN):
                    gn = gni(tc, dc, n)
                    if n + 1 < NN:
                        if n >= 1:
                            wge(v, "scv", fscan[n - 1])
                        fdx[n + 1] = inc(v.tensor_tensor(
                            dxD[(n + 1) % 2][:], uD[:],
                            bb_sb[:, (n + 1) * TW:(n + 2) * TW], op=OP.mult), "vf")
                    wge(v, "da", gn)
                    wge(v, "vf", fdx[n])
                    last = v.tensor_tensor_scan(h_band[:, n * TW:(n + 1) * TW],
                                                daD[n % 2][:], dxD[n % 2][:],
                                                hcol[:, dc * NN + n: dc * NN + n + 1],
                                                op0=OP.mult, op1=OP.add)
                    fscan[n] = inc(last, "scv")
                wge(v, "scv", fscan[NN - 1])
                fcarry = inc(v.tensor_copy(
                    hcol[:, dc * NN:(dc + 1) * NN],
                    h_band[:].rearrange("p (n w) -> p n w", n=NN)[:, :, TW - 1:TW]), "vf")
                wge(v, "vf", fcarry)
                fg = [0] * NN
                for n in range(NN):
                    fg[n] = inc(v.tensor_tensor(h_band[:, n * TW:(n + 1) * TW],
                                                h_band[:, n * TW:(n + 1) * TW],
                                                cb_sb[:, n * TW:(n + 1) * TW],
                                                op=OP.mult), "vf")
                fl1 = [0] * 8
                for i8 in range(8):
                    wge(v, "vf", fg[2 * i8 + 1])
                    if i8 >= 1:
                        wge(v, "vf", fl1[i8 // 2])
                    fl1[i8] = inc(v.tensor_tensor(
                        h_band[:, i8 * TW:(i8 + 1) * TW],
                        h_band[:, 2 * i8 * TW:(2 * i8 + 1) * TW],
                        h_band[:, (2 * i8 + 1) * TW:(2 * i8 + 2) * TW], op=OP.add), "vf")
                fl2 = [0] * 4
                for i4 in range(4):
                    wge(v, "vf", fl1[2 * i4 + 1])
                    if i4 >= 1:
                        wge(v, "vf", fl2[i4 // 2])
                    fl2[i4] = inc(v.tensor_tensor(
                        h_band[:, i4 * TW:(i4 + 1) * TW],
                        h_band[:, 2 * i4 * TW:(2 * i4 + 1) * TW],
                        h_band[:, (2 * i4 + 1) * TW:(2 * i4 + 2) * TW], op=OP.add), "vf")
                wge(v, "vf", fl2[1])
                f30 = inc(v.tensor_tensor(h_band[:, 0:TW], h_band[:, 0:TW],
                                          h_band[:, TW:2 * TW], op=OP.add), "vf")
                wge(v, "vf", fl2[3])
                f31 = inc(v.tensor_tensor(y_sl[:], h_band[:, 2 * TW:3 * TW],
                                          h_band[:, 3 * TW:4 * TW], op=OP.add), "vf")
                wge(v, "vf", f31)
                fy = inc(v.tensor_tensor(y_sl[:], y_sl[:], h_band[:, 0:TW],
                                         op=OP.add), "vf")
                wge(v, "szs", tc * DC + dc + 1)
                wge(v, "vf", fy)
                fyy = inc(v.scalar_tensor_tensor(y_sl[:],
                                                 xc_tile(dc)[:, tc * TW:(tc + 1) * TW],
                                                 c_dskip[:, dc:dc + 1], y_sl[:],
                                                 op0=OP.mult, op1=OP.add), "vf")
                wge(v, "vf", fyy)
                last = v.tensor_tensor(xc_tile(dc)[:, tc * TW:(tc + 1) * TW],
                                       y_sl[:], szall[:, dc * TW:(dc + 1) * TW],
                                       op=OP.mult)
                inc(last, "gy")
            for q in range(4):
                i = epidx(tc, q)
                b = q % 2
                ps = psA if q < 2 else psB
                pso = ps[:, (q % 2) * 1024:(q % 2) * 1024 + 1024]
                wge(v, "ot", 16 * (tc + 1))
                if i >= 2:
                    wge(v, "sx0" if i % 2 == 0 else "sx1", QS_X1SP[i - 2])
                inc(v.tensor_reduce(epD["sum"][:, b:b + 1], pso, axis=AX.X,
                                    op=OP.add), "epv")
                wge(v, "ep2", 6 * i + 2)
                inc(v.scalar_tensor_tensor(epD["var"][:, b:b + 1],
                                           epD["ssq"][:, b:b + 1], 1.0 / D,
                                           epD["m2"][:, b:b + 1],
                                           op0=OP.mult, op1=OP.subtract), "epv")
                wge(v, "ep2", 6 * i + 3)
                fmn = inc(v.tensor_scalar(sjunk[:], pso, epD["nm"][:, b:b + 1],
                                          epD["r"][:, b:b + 1],
                                          op0=OP.add, op1=OP.mult), "vf")
                wge(v, "vf", fmn)
                wge(v, "xr", QS_RELOAD[i])
                inc(v.tensor_tensor(x1tD[b][:], sjunk[:], xreD[:], op=OP.add), "epv")
                wge(v, "epv", 6 * i + 3)
                inc(v.tensor_reduce(epD["sum2"][:, b:b + 1], x1tD[b][:], axis=AX.X,
                                    op=OP.add), "epv")
                wge(v, "ep2", 6 * i + 5)
                inc(v.scalar_tensor_tensor(epD["var2"][:, b:b + 1],
                                           epD["ssq2"][:, b:b + 1], 1.0 / D,
                                           epD["m22"][:, b:b + 1],
                                           op0=OP.mult, op1=OP.subtract), "epv")
                wge(v, "ep2", 6 * i + 6)
                if i >= 1:
                    wge(v, "ssr", QS_SRCSP[i - 1])
                inc(v.tensor_scalar(srcbD[:], x1tD[b][:], epD["nm2"][:, b:b + 1],
                                    epD["r2"][:, b:b + 1],
                                    op0=OP.add, op1=OP.mult), "epv")

    esSD.close()
    esA.close()
    if stop == "D":
        @block.sync
        def _(sync):
            for nm in ["qc", "qga", "qcx", "ax0", "ax1", "bw0", "bw1", "gb",
                       "wz0", "wz1", "wo0", "wo1", "wo2", "xps", "sx0", "sx1",
                       "ssr", "xr"]:
                wge(sync, nm, cnt[nm])
            wge(sync, "scv", cnt["scv"])
            wge(sync, "epv", cnt["epv"])
            wge(sync, "ot", cnt["ot"])
        es.close()
        return nc, tap_handles

    mksem("pc", "mh", "hr", "os0", "os1", "eq", "eqr", "eq0", "eq1", "eq2", "eq3", "fw0", "fw1", "hq", "oq0", "oq1", "po0", "po1", "ow0", "ow1",
          "hw0", "hw1", "hq2", "outq0", "outq1", "asp")
    srcT = ln0T
    role = xc_tile

    esE = ExitStack()
    psE = esE.enter_context(nc.psum_tensor("psE", [128, 2048], BF16))
    psF1 = esE.enter_context(nc.psum_tensor("psF1", [128, 2048], F32))
    psF2 = esE.enter_context(nc.psum_tensor("psF2", [128, 1024], F32))

    gv_tm = sbt("gv_tm", [128, S], BF16)
    gvo_f = sbt("gvo_f", [128, S], BF16)
    attn_acc = sbt("attn_acc", [128, 8 * D], F32)

    esEs = ExitStack()
    srcre = [esEs.enter_context(nc.sbuf_tensor(f"srcreE{i}", [128, 2 * D], BF16))
             for i in range(2)]
    E_T0, E_S0 = cnt["t"], cnt["s"]

    @block.gpsimd
    def _(gp):
        wge(gp, "ot", 64)
        wge(gp, "epv", 6 * 16)
        wge(gp, "ssr", cnt["ssr"])
        for st in range(16):
            if st >= 4:
                wge(gp, "s", E_S0 + st - 3)
            dma(gp, f"eq{st % 4}", srcre[(st // 2) % 2][:, (st % 2) * D:(st % 2 + 1) * D],
                d_srcsp[st * 128:(st + 1) * 128, :])

    @block.tensor
    def _(t):
        for st in range(16):
            wge(t, f"eq{st % 4}", 16 * (st // 4 + 1))
            if st >= 2:
                wge(t, "s", E_S0 + st - 1)
            src = srcre[(st // 2) % 2][:, (st % 2) * D:(st % 2 + 1) * D]
            half = (st % 2) * 1024
            for j in range(8):
                last = t.transpose(psE[:, half + j * 128: half + (j + 1) * 128],
                                   src[:, j * 128:(j + 1) * 128], c_id[:])
            inc(last, "t")

    @block.scalar
    def _(s):
        for st in range(16):
            wge(s, "t", E_T0 + st + 1)
            half = (st % 2) * 1024
            dst = bass.AP(act_a[:].tensor, act_a[:].offset + st * 128,
                          [act_a[:].ap[0], [S, 8], [1, 128]])
            inc(s.copy(dst, psE[:, half:half + 1024].rearrange(
                "p (j c) -> p j c", j=8)), "s")
    E_SEV_END = cnt["s"]
    esEs.close()

    esFq = ExitStack()
    wqkF = [esFq.enter_context(nc.sbuf_tensor(f"wqkF{i}", [128, 1024], BF16))
            for i in range(2)]
    wselF = [esFq.enter_context(nc.sbuf_tensor(f"wselF{i}", [128, 1024], BF16))
             for i in range(2)]
    F_T0, F_S0 = cnt["t"], cnt["s"]
    QG_QK = [0] * 16

    @block.gpsimd
    def _(gp):
        wge(gp, "ot", 64)
        wge(gp, "t", F_T0)
        dma(gp, "qcx", wselF[0][:], d_wsel[0])
        dma(gp, "qcx", wselF[1][:], d_wsel[1])
        for m in range(16):
            if m >= 2:
                wge(gp, "t", F_T0 + 2 * (m - 1))
            QG_QK[m] = dma(gp, "fw0" if m % 2 == 0 else "fw1", wqkF[m % 2][:], d_wqk[m])

    @block.tensor
    def _(t):
        wge(t, "s", E_SEV_END)
        for m in range(16):
            wge(t, "fw0" if m % 2 == 0 else "fw1", QG_QK[m])
            for half in range(2):
                hm = 2 * m + half
                if hm >= 2:
                    wge(t, "s", F_S0 + hm - 1)
                for n in range(2):
                    nn = half * 2 + n
                    ps = psF1[:, (half * 2 + n) * 512:(half * 2 + n + 1) * 512]
                    for k in range(8):
                        last = t.matmul(ps, wqkF[m % 2][:, k * 128:(k + 1) * 128],
                                        srcT(k)[:, nn * 512:(nn + 1) * 512],
                                        start=(k == 0), stop=(k == 7))
                inc(last, "t")

    @block.scalar
    def _(s):
        for m in range(16):
            for half in range(2):
                hm = 2 * m + half
                wge(s, "t", F_T0 + hm + 1)
                for n in range(2):
                    nn = half * 2 + n
                    ps = psF1[:, (half * 2 + n) * 512:(half * 2 + n + 1) * 512]
                    last = s.copy(role(m)[:, nn * 512:(nn + 1) * 512], ps)
                inc(last, "s")
    F_QK_EVAC = cnt["s"]

    esR = ExitStack()
    cosR = esR.enter_context(nc.sbuf_tensor("cosR", [128, S], F32))
    sinR = esR.enter_context(nc.sbuf_tensor("sinR", [128, S], F32))
    rsc = [esR.enter_context(nc.sbuf_tensor(f"rscR{i}", [128, S], F32))
           for i in range(3)]

    @block.sync
    def _(sync):
        wge(sync, "t", F_T0)
        dma(sync, "eqr", cosR[:], d_cosr)
        dma(sync, "eqr", sinR[:], d_sinr)
    R_EQ = cnt["eqr"]
    F2_VB = cnt["v"]

    @block.vector
    def _(v):
        wge(v, "eqr", R_EQ)
        wge(v, "s", F_QK_EVAC)
        for ri, (a, b2) in enumerate([(0, 2), (1, 3), (4, 6), (5, 7)]):
            if ri >= 1:
                wge(v, "v", F2_VB + ri)
            inc(v.tensor_tensor(rsc[0][:], role(a), cosR[:], op=OP.mult), "vf")
            inc(v.tensor_tensor(rsc[1][:], role(a), sinR[:], op=OP.mult), "vf")
            f3 = inc(v.tensor_tensor(rsc[2][:], role(b2), sinR[:], op=OP.mult), "vf")
            wge(v, "vf", f3)
            f4 = inc(v.tensor_tensor(role(a), rsc[0][:], rsc[2][:], op=OP.subtract), "vf")
            wge(v, "vf", f4)
            f5 = inc(v.tensor_tensor(rsc[0][:], role(b2), cosR[:], op=OP.mult), "vf")
            wge(v, "vf", f5)
            inc(v.tensor_tensor(role(b2), rsc[1][:], rsc[0][:], op=OP.add), "v")
    F_ROPE_V = cnt["v"]
    esR.close()

    esG = ExitStack()
    sbg = lambda name, shape, dt: esG.enter_context(nc.sbuf_tensor(name, list(shape), dt))
    glv_f = sbg("glv_f", [128, S], BF16)
    glo_f = sbg("glo_f", [128, S], BF16)
    sgv_f = sbg("sgv_f", [128, S], BF16)
    sgo_f = sbg("sgo_f", [128, S], BF16)
    glv_tm = sbg("glv_tm", [128, S], BF16)
    glo_tm = sbg("glo_tm", [128, S], BF16)
    sgv_tm = sbg("sgv_tm", [128, S], BF16)
    mo_tm = sbg("mo_tm", [128, S], BF16)
    mo_f = sbg("mo_f", [128, S], BF16)
    gsc = sbg("gsc", [128, S], BF16)
    gm1 = sbg("gm1", [128, 256], F32)
    gm2 = sbg("gm2", [128, 256], F32)

    G_T0, G_S0, G_V0 = cnt["t"], cnt["s"], cnt["v"]
    G_QCX0 = cnt["qcx"]

    G_QCX = cnt["qcx"]

    @block.tensor
    def _(t):
        wge(t, "qcx", G_QCX)
        for m in range(2):
            for n in range(4):
                ps = psF2[:, (n % 2) * 512:(n % 2) * 512 + 512]
                if 4 * m + n >= 2:
                    wge(t, "s", G_S0 + 4 * m + n - 1)
                for k in range(8):
                    last = t.matmul(ps, wselF[m][:, k * 128:(k + 1) * 128],
                                    srcT(k)[:, n * 512:(n + 1) * 512],
                                    start=(k == 0), stop=(k == 7))
                inc(last, "t")

    @block.scalar
    def _(s):
        wge(s, "v", F_ROPE_V)
        for m in range(2):
            gl = glv_f if m == 0 else glo_f
            sg = sgv_f if m == 0 else sgo_f
            for n in range(4):
                wge(s, "t", G_T0 + 4 * m + n + 1)
                ps = psF2[:, (n % 2) * 512:(n % 2) * 512 + 512]
                s.copy(gl[:, n * 512:(n + 1) * 512], ps)
                inc(s.activation(sg[:, n * 512:(n + 1) * 512], ps, AF.Sigmoid), "s")
    G_SEL_EVAC = cnt["s"]

    GT_T0, GT_S0 = cnt["t"], cnt["s"]
    @block.tensor
    def _(t):
        wge(t, "s", G_SEL_EVAC)
        rr = 0
        for srcten in [glv_f, glo_f, sgv_f]:
            for blk in range(2):
                if rr >= 2:
                    wge(t, "s", GT_S0 + rr - 1)
                half = (rr % 2) * 1024
                for i in range(8):
                    st = blk * 8 + i
                    last = t.transpose(psE[:, half + i * 128: half + (i + 1) * 128],
                                       srcten[:, st * 128:(st + 1) * 128], c_id[:])
                inc(last, "t")
                rr += 1

    @block.scalar
    def _(s):
        rr = 0
        for dstten in [glv_tm, glo_tm, sgv_tm]:
            for blk in range(2):
                wge(s, "t", GT_T0 + rr + 1)
                half = (rr % 2) * 1024
                inc(s.copy(dstten[:, blk * 1024:(blk + 1) * 1024],
                           psE[:, half:half + 1024]), "s")
                rr += 1
    GT_SEV = cnt["s"]

    def g3(ap):
        return ap.rearrange("p (a e) -> p a e", e=8)
    TK_V0 = cnt["v"]
    @block.vector
    def _(v):
        wge(v, "s", GT_SEV)
        for (glt, mout) in [(glv_tm, None), (glo_tm, mo_tm)]:
            f1 = inc(v.tensor_reduce(gm1[:], g3(glt[:]), axis=AX.X, op=OP.max), "vf")
            wge(v, "vf", f1)
            m1b = bass.AP(gm1[:].tensor, gm1[:].offset,
                          [gm1[:].ap[0], [1, 256], [0, 8]])
            f2 = inc(v.tensor_tensor(g3(gsc[:]), g3(glt[:]), m1b, op=OP.is_ge), "vf")
            wge(v, "vf", f2)
            f3 = inc(v.scalar_tensor_tensor(gsc[:], gsc[:], -1e30, glt[:],
                                            op0=OP.mult, op1=OP.add), "vf")
            wge(v, "vf", f3)
            f4 = inc(v.tensor_reduce(gm2[:], g3(gsc[:]), axis=AX.X, op=OP.max), "vf")
            wge(v, "vf", f4)
            m2b = bass.AP(gm2[:].tensor, gm2[:].offset,
                          [gm2[:].ap[0], [1, 256], [0, 8]])
            if mout is None:
                f5 = inc(v.tensor_tensor(g3(gsc[:]), g3(glt[:]), m2b, op=OP.is_ge), "vf")
                wge(v, "vf", f5)
                inc(v.tensor_tensor(gv_tm[:], gsc[:], sgv_tm[:], op=OP.mult), "v")
            else:
                inc(v.tensor_tensor(g3(mout[:]), g3(glt[:]), m2b, op=OP.is_ge), "v")
    TK_V_END = cnt["v"]

    MT_T0, MT_S0 = cnt["t"], cnt["s"]
    @block.tensor
    def _(t):
        wge(t, "v", TK_V_END)
        for blk in range(2):
            if blk >= 2:
                wge(t, "s", MT_S0 + blk - 1)
            half = (blk % 2) * 1024
            for i in range(8):
                st = blk * 8 + i
                last = t.transpose(psE[:, half + i * 128: half + (i + 1) * 128],
                                   mo_tm[:, st * 128:(st + 1) * 128], c_id[:])
            inc(last, "t")

    @block.scalar
    def _(s):
        for blk in range(2):
            wge(s, "t", MT_T0 + blk + 1)
            half = (blk % 2) * 1024
            inc(s.copy(mo_f[:, blk * 1024:(blk + 1) * 1024],
                       psE[:, half:half + 1024]), "s")
    MT_SEV = cnt["s"]

    @block.vector
    def _(v):
        wge(v, "s", MT_SEV)
        inc(v.tensor_tensor(gvo_f[:], sgo_f[:], mo_f[:], op=OP.mult), "v")
    GVO_DONE = cnt["v"]
    esG.close()

    @block.gpsimd
    def _(gp):
        dma(gp, "pc", d_w1b, d_w1)
        dma(gp, "pc", d_w2b, d_w2)
        dma(gp, "pc", d_wob, d_wo)
    PC_ALL = cnt["pc"]

    esN = ExitStack()
    sq_t = [esN.enter_context(nc.sbuf_tensor(f"sq_t{i}", [128, S], BF16))
            for i in range(2)]
    nrm = esN.enter_context(nc.sbuf_tensor("nrm", [128, S], F32))
    nmk = esN.enter_context(nc.sbuf_tensor("nmk", [128, 1], F32))
    pats = []
    for ti in range(2):
        p = np.zeros((128, 16), dtype=bf16)
        for r in range(128):
            p[r, ti * 8 + r // 16] = 1
        pats.append(nc.inline_tensor(p, name=f"patN16_{ti}"))
    for ti in range(4):
        p = np.zeros((128, 16), dtype=bf16)
        for r in range(128):
            p[r, ti * 4 + r // 32] = 1
        pats.append(nc.inline_tensor(p, name=f"patN32_{ti}"))
    c_pat = [esN.enter_context(nc.sbuf_tensor(f"c_patN{i}", [128, 16], BF16))
             for i in range(6)]

    @block.sync
    def _(sync):
        wge(sync, "v", GVO_DONE)
        for i in range(6):
            dma(sync, "eq", c_pat[i][:], pats[i].ap())
    N_EQ = cnt["eq"]

    nrm_q = [(0, 0), (1, 1), (2, 0), (3, 1), (8, 2), (9, 3), (10, 4), (11, 5)]
    nrm_k = [(4, 0), (5, 1), (6, 0), (7, 1), (12, 2), (13, 3), (14, 4), (15, 5)]
    N_S0, N_T0, N_V0 = cnt["s"], cnt["t"], cnt["v"]

    @block.scalar
    def _(s):
        wge(s, "v", GVO_DONE)
        for i, (r, _) in enumerate(nrm_q):
            if i >= 2:
                wge(s, "t", N_T0 + i - 1)
            inc(s.activation(sq_t[i % 2][:], role(r), AF.Square), "s")
        wge(s, "t", N_T0 + 8)
        inc(s.activation(nrm[0:16, :], psF1[0:16, :], AF.Ln), "s")
        for j, (r, _) in enumerate(nrm_k):
            i = 8 + j
            wge(s, "t", N_T0 + i - 1)
            inc(s.activation(sq_t[i % 2][:], role(r), AF.Square), "s")
        wge(s, "t", N_T0 + 16)
        wge(s, "v", N_V0 + 1)
        last = s.activation(nmk[0:16, :], nmk[0:16, :], AF.Ln)
        wge(s, "sf", inc(last, "sf"))
        last = s.activation(nmk[0:16, :], nmk[0:16, :], AF.Copy, scale=0.5)
        wge(s, "sf", inc(last, "sf"))
        inc(s.activation(nrm[32:48, :], nrm[0:16, :], AF.Exp, scale=0.5,
                         bias=nmk[0:16, :]), "s")

    @block.tensor
    def _(t):
        wge(t, "eq", N_EQ)
        for i, (r, pat) in enumerate(nrm_q + nrm_k):
            wge(t, "s", N_S0 + i + 1 + (1 if i >= 8 else 0))
            first, laststep = (i % 8 == 0), (i % 8 == 7)
            for n in range(4):
                out = bass.AP(psF1[:].tensor, psF1[:].offset + n * 512,
                              [[psF1[:].ap[0][0], 16], [1, 512]])
                last = t.matmul(out, c_pat[pat][:], sq_t[i % 2][:, n * 512:(n + 1) * 512],
                                start=first, stop=laststep)
            inc(last, "t")

    @block.vector
    def _(v):
        wge(v, "t", N_T0 + 16)
        inc(v.tensor_reduce(nmk[0:16, :], psF1[0:16, :], axis=AX.X, op=OP.max), "v")
        wge(v, "s", N_S0 + 18)
        inc(v.tensor_scalar(sq_t[0][0:16, :], nrm[32:48, :], -1.0, None,
                            op0=OP.mult), "v")
    N_MHAT_V = cnt["v"]

    @block.sync
    def _(sync):
        wge(sync, "v", N_MHAT_V)
        dma(sync, "mh", d_mhat, sq_t[0][0:16, :])
    esN.close()
    esFq.close()

    esH1 = ExitStack()
    sbh = lambda name, shape, dt: esH1.enter_context(nc.sbuf_tensor(name, list(shape), dt))
    qT_aug = sbh("qT_aug", [128, S], BF16)
    kT_aug = sbh("kT_aug", [128, S], BF16)
    v_aug = sbh("v_aug", [128, 16 * 66], BF16)
    attT = sbh("attT", [128, 16 * 512], BF16)
    wvG = [sbh(f"wvG{i}", [128, 4 * 512], BF16) for i in range(2)]
    ost2 = [sbh(f"ost2_{i}", [64, 512], BF16) for i in range(2)]
    lrow = sbh("lrow", [128, 512], BF16)
    rlb = [sbh(f"rlbG{i}", [64, 512], BF16) for i in range(2)]
    vgt = sbh("vgt", [128, 512], BF16)

    mksem("gt", "gs", "gl", "gv2", "og", "ov")
    G2_T0, G2_S0, G2_V0 = cnt["gt"], cnt["gs"], cnt["gv2"]
    QCW, NQC = 512, 4
    GT_H = 16 + NQC * 9
    GS_H = NQC * 8
    GV_H = 16 + NQC
    gtv = lambda h, st: GT_H * h + st + 1
    gtsc = lambda h, qc, grp: GT_H * h + 16 + 9 * qc + grp + 1
    gtav = lambda h, qc: GT_H * h + 16 + 9 * qc + 9
    gvg = lambda h, st: GV_H * h + st + 1
    gvo = lambda h, qc: GV_H * h + 16 + qc + 1
    gli = lambda h, qc: NQC * h + qc + 1

    QG_WV = [0] * 16
    HQ_TICK = [0] * 16
    OQ_TICK = [[0] * NQC for _ in range(16)]

    QG_WV2 = [[0, 0] for _ in range(16)]
    @block.gpsimd
    def _(gp):
        wge(gp, "sx0", cnt["sx0"])
        wge(gp, "sx1", cnt["sx1"])
        wge(gp, "v", N_MHAT_V)
        wge(gp, "t", N_T0 + 16)
        wge(gp, "s", N_S0 + 18)

        def wv_dma(h):
            for half in range(2):
                g = 2 * h + half
                if g >= 2:
                    wge(gp, "gt", gtv((g - 2) // 2, 15))
                QG_WV2[h][half] = dma(gp, "fw0" if g % 2 == 0 else "fw1",
                                      wvG[g % 2][:], d_wv[h, half * 4:(half + 1) * 4])
        wv_dma(0)
        wv_dma(1)
        for h in range(16):
            for qc in range(NQC):
                u = h * NQC + qc
                wge(gp, "gl", gli(h, qc))
                if u >= 2:
                    u2 = u - 2
                    wge(gp, "gv2", gvo(u2 // NQC, u2 % NQC))
                OQ_TICK[h][qc] = dma(gp, "oq0" if u % 2 == 0 else "oq1",
                                     rlb[u % 2][:], repl_ap(lrow[0:1, :], QCW,
                                                            nparts=64))
            if h + 2 < 16:
                wv_dma(h + 2)

    OSP_TICK = [0] * (16 * NQC)
    @block.sync
    def _(sync):
        wge(sync, "v", N_MHAT_V)
        wge(sync, "mh", 16)

        def o_spill(h):
            for qc in range(NQC):
                u = h * NQC + qc
                wge(sync, "gv2", G2_V0 + gvo(h, qc))
                OSP_TICK[u] = dma(sync, "os0" if u % 2 == 0 else "os1",
                                  d_osp[h, :, qc * QCW:(qc + 1) * QCW],
                                  ost2[u % 2][:])
        for h in range(16):
            if h >= 1:
                wge(sync, "gt", GT_H * h)
            src_q = [(role(0 + h // 8), (h % 8) * 16, 0, 16),
                     (role(2 + h // 8), (h % 8) * 16, 16, 16),
                     (role(8 + h // 4), (h % 4) * 32, 32, 32)]
            src_k = [(role(4 + h // 8), (h % 8) * 16, 0, 16),
                     (role(6 + h // 8), (h % 8) * 16, 16, 16),
                     (role(12 + h // 4), (h % 4) * 32, 32, 32)]
            for (srt, r0, d0, nr) in src_q:
                dma(sync, "hq", qT_aug[d0:d0 + nr, :],
                    bass.AP(srt.tensor, srt.offset + r0 * srt.ap[0][0],
                            [[srt.ap[0][0], nr], [1, S]]))
            dma(sync, "hq", qT_aug[64:65, :], d_mhat[h:h + 1, :])
            for (srt, r0, d0, nr) in src_k:
                dma(sync, "hq", kT_aug[d0:d0 + nr, :],
                    bass.AP(srt.tensor, srt.offset + r0 * srt.ap[0][0],
                            [[srt.ap[0][0], nr], [1, S]]))
            HQ_TICK[h] = cnt["hq"]
            if h >= 1:
                o_spill(h - 1)
        o_spill(15)

    @block.vector
    def _(v):
        wge(v, "mh", 16)
        v.memset(kT_aug[64:65, :], 1.0)
        inc(v.memset(v_aug[:].rearrange("p (t c) -> p t c", c=66)[:, :, 64:65], 1.0),
            "v")
    G_ONES_V = cnt["v"]

    @block.tensor
    def _(t):
        wge(t, "v", G_ONES_V)
        for h in range(16):
            wge(t, "fw0" if (2 * h) % 2 == 0 else "fw1", QG_WV2[h][0])
            wge(t, "fw0" if (2 * h + 1) % 2 == 0 else "fw1", QG_WV2[h][1])
            wge(t, "hq", HQ_TICK[h])
            for st in range(16):
                if h + st > 0:
                    if st < 2:
                        wge(t, "gv2", G2_V0 + GV_H * h)
                    else:
                        wge(t, "gv2", G2_V0 + GV_H * h + st - 1)
                ps = psF2[:, (st % 2) * 512:(st % 2) * 512 + 512]
                for k in range(8):
                    last = t.matmul(ps, srcT(k)[:, st * 128:(st + 1) * 128],
                                    wvG[(2 * h + k // 4) % 2][:, (k % 4) * 512:(k % 4 + 1) * 512],
                                    start=(k == 0), stop=(k == 7))
                inc(last, "gt")
            for qc in range(NQC):
                for grp in range(8):
                    gq = 8 * qc + grp
                    if gq >= 2:
                        wge(t, "gs", G2_S0 + GS_H * h + gq - 1)
                    elif h >= 1:
                        wge(t, "gs", G2_S0 + GS_H * h - 1 + gq)
                    slot = (grp % 2) * 1024
                    for ti in range(2):
                        tt = grp * 2 + ti
                        last = t.matmul(psF1[:, slot + ti * 512: slot + (ti + 1) * 512],
                                        kT_aug[0:65, tt * 128:(tt + 1) * 128],
                                        qT_aug[0:65, qc * QCW:(qc + 1) * QCW],
                                        start=True, stop=True)
                    inc(last, "gt")
                wge(t, "gs", G2_S0 + GS_H * h + 8 * (qc + 1))
                if qc >= 2:
                    wge(t, "gv2", G2_V0 + GV_H * h + 16 + qc - 1)
                else:
                    wge(t, "gv2", G2_V0 + GV_H * h + 16)
                po = bass.AP(psF2[:].tensor, psF2[:].offset + (qc % 2) * 512,
                             [[psF2[:].ap[0][0], 65], [1, QCW]])
                for tt in range(16):
                    last = t.matmul(po, v_aug[:, tt * 66: tt * 66 + 65],
                                    attT[:, tt * 512:(tt + 1) * 512],
                                    start=(tt == 0), stop=(tt == 15))
                inc(last, "gt")

    @block.scalar
    def _(s):
        for h in range(16):
            for qc in range(NQC):
                for grp in range(8):
                    wge(s, "gt", G2_T0 + gtsc(h, qc, grp))
                    slot = (grp % 2) * 1024
                    inc(s.activation(attT[:, (2 * grp) * 512:(2 * grp + 2) * 512],
                                     psF1[:, slot:slot + 1024], AF.Exp,
                                     scale=0.125), "gs")
                wge(s, "gt", G2_T0 + gtav(h, qc))
                u = h * NQC + qc
                if u >= 1:
                    up = u - 1
                    wge(s, "oq0" if up % 2 == 0 else "oq1",
                        OQ_TICK[up // NQC][up % NQC])
                poap = bass.AP(psF2[:].tensor,
                               psF2[:].offset + (qc % 2) * 512 + 64 * psF2[:].ap[0][0],
                               [[psF2[:].ap[0][0], 1], [1, QCW]])
                last = s.copy(lrow[0:1, :], poap)
                wge(s, "sf", inc(last, "sf"))
                last = s.activation(lrow[0:1, :], lrow[0:1, :], AF.Ln)
                wge(s, "sf", inc(last, "sf"))
                inc(s.activation(lrow[0:1, :], lrow[0:1, :], AF.Exp, scale=-1.0), "gl")

    vgr = [0]
    @block.vector
    def _(v):
        wge(v, "s", GT_SEV)
        for h in range(16):
            if h >= 1:
                wge(v, "gt", G2_T0 + GT_H * h)
            for st in range(16):
                wge(v, "gt", G2_T0 + gtv(h, st))
                if h + st > 0:
                    wge(v, "gv2", vgr[0])
                ps = psF2[:, (st % 2) * 512:(st % 2) * 512 + 512]
                gvap = bass.AP(gv_tm[:].tensor, gv_tm[:].offset + st * 128 + h * 8,
                               [gv_tm[:].ap[0], [0, 64], [1, 8]])
                vg = vgt[:]
                f1 = inc(v.tensor_tensor(vg.rearrange("p (a e) -> p a e", e=8),
                                         ps.rearrange("p (a e) -> p a e", e=8),
                                         gvap, op=OP.mult), "vf")
                wge(v, "vf", f1)
                with nc.allow_low_precision(reason="8-term expert sum, fp32 internal"):
                    vgr[0] = inc(v.tensor_reduce(v_aug[:, st * 66: st * 66 + 64],
                                        vg.rearrange("p (a e) -> p a e", e=8),
                                        axis=AX.X, op=OP.add), "gv2")
            for qc in range(NQC):
                u = h * NQC + qc
                wge(v, "gt", G2_T0 + gtav(h, qc))
                wge(v, "oq0" if u % 2 == 0 else "oq1", OQ_TICK[h][qc])
                if u >= 2:
                    wge(v, "os0" if u % 2 == 0 else "os1", OSP_TICK[u - 2])
                po64 = bass.AP(psF2[:].tensor, psF2[:].offset + (qc % 2) * 512,
                               [[psF2[:].ap[0][0], 64], [1, QCW]])
                inc(v.tensor_tensor(ost2[u % 2][:], po64, rlb[u % 2][:],
                                    op=OP.mult), "gv2")
    G_HEADS_V = cnt["gv2"]
    esH1.close()

    mksem("e2v", "e2s", "e2t", "e3s", "ht", "hs", "hv", "ar", "xr2",
          "f2v", "f2s")
    HW2 = 1024
    o2sl = lambda j: xc_a[:, j * HW2:(j + 1) * HW2]
    ostack = xc_a[:, 8 * HW2:9 * HW2]
    pairst = lambda slot: bass.AP(xc_a[:].tensor, xc_a[:].offset + 9 * HW2 + slot * S,
                                  [[xc_a[:].ap[0][0], 2], [1, S]])
    wosl = lambda j: xc_a[:, 16384 + j * 1024: 16384 + (j + 1) * 1024]
    esO = ExitStack()
    srcrl = [esO.enter_context(nc.sbuf_tensor(f"srcrlO{i}", [128, D], BF16))
             for i in range(2)]
    aln_tm = [esO.enter_context(nc.sbuf_tensor(f"alnO{i}", [128, D], BF16))
              for i in range(8)]
    g1b = esO.enter_context(nc.sbuf_tensor("g1b", [128, D], F32))
    b1b = esO.enter_context(nc.sbuf_tensor("b1b", [128, D], F32))
    epO = {nm: esO.enter_context(nc.sbuf_tensor("epO_" + nm, [128, 2], F32))
           for nm in ["sum", "nm", "m2", "ssq", "var", "lv", "r"]}

    OG_R = 16
    ogGb = lambda g: (g // 8) * OG_R + (g % 8) + 1
    ogSt = lambda sh, r, st8: (sh * 8 + r) * OG_R + 8 + st8 + 1
    ovO2 = lambda g: (g // 8) * OG_R + (g % 8) + 1
    ovAdd = lambda sh, r, st8: (sh * 8 + r) * OG_R + 8 + st8 + 1
    OW_TICK = [[0] * 64 for _ in range(2)]
    PO_TICK = [[0] * 64 for _ in range(2)]
    OS_TICK = [[0] * 16 for _ in range(2)]

    HQ2_SRC = [0] * 16
    @block.gpsimd
    def _(gp):
        wge(gp, "pc", PC_ALL)
        for sh in range(2):
            for u in range(64):
                gk = sh * 64 + u
                if gk == 0:
                    wge(gp, "gt", GT_H * 16)
                if gk >= 8:
                    p = gk - 8
                    wge(gp, "og", ogSt(p // 64, (p % 64) // 8, 7))
                OW_TICK[sh][u] = dma(gp, "ow0" if u % 2 == 0 else "ow1",
                                     wosl(u % 8)[:, :], d_wob[u])
            for s8 in range(8):
                i = sh * 8 + s8
                wge(gp, "ov", ovAdd(sh, 7, s8))
                if i >= 2:
                    wge(gp, "e2v", 5 * (i - 2) + 1)
                HQ2_SRC[i] = dma(gp, "hr", srcrl[i % 2][:],
                                 d_srcsp[i * 128:(i + 1) * 128, :])
                if stop == "G":
                    wge(gp, "e2v", 5 * i + 1)
                    wge(gp, "asp", cnt["asp"])
                    dma(gp, "asp", d_out[i * 128:(i + 1) * 128, :],
                        attn_acc[:, s8 * D:(s8 + 1) * D])
                wge(gp, "e2v", 5 * i + 4)
                wge(gp, "asp", cnt["asp"])
                dma(gp, "asp", d_asp[i * 128:(i + 1) * 128, :],
                    aln_tm[i % 8][:])

    @block.sync
    def _(sync):
        wge(sync, "gt", GT_H * 16)
        wge(sync, "v", GVO_DONE)
        wge(sync, "gv2", G_HEADS_V)
        wge(sync, "oq0", cnt["oq0"])
        wge(sync, "oq1", cnt["oq1"])
        wge(sync, "os0", cnt["os0"])
        wge(sync, "os1", cnt["os1"])
        dma(sync, "hq2", g1b[:], repl_ap(d_lng1, D))
        dma(sync, "hq2", b1b[:], repl_ap(d_lnb1, D))
        for sh in range(2):
            for u in range(64):
                g = sh * 64 + u
                h, pr = u // 4, u % 4
                if pr == 0:
                    if g >= 1:
                        wge(sync, "ov", ovO2(g - 1))
                    srcap = d_osp[h, :, sh * HW2:(sh + 1) * HW2]
                    dma(sync, "hq", bass.AP(ostack.tensor, ostack.offset,
                                            [[ostack.ap[0][0], 64], [1, HW2]]), srcap)
                    dma(sync, "hq", bass.AP(ostack.tensor,
                                            ostack.offset + 64 * ostack.ap[0][0],
                                            [[ostack.ap[0][0], 64], [1, HW2]]), srcap)
                    OS_TICK[sh][h] = cnt["hq"]
                if g >= 2:
                    wge(sync, "og", ogGb(g - 2))
                PO_TICK[sh][u] = dma(
                    sync, "po0" if g % 2 == 0 else "po1", pairst(g % 2),
                    bass.AP(gvo_f[:].tensor,
                            gvo_f[:].offset + (h * 8 + 2 * pr) * gvo_f[:].ap[0][0],
                            [[gvo_f[:].ap[0][0], 2], [1, S]]))


    @block.tensor
    def _(t):
        wge(t, "gv2", G_HEADS_V)
        for sh in range(2):
            for r in range(8):
                for ui in range(8):
                    u = r * 8 + ui
                    g = sh * 64 + u
                    wge(t, "po0" if g % 2 == 0 else "po1", PO_TICK[sh][u])
                    if u % 4 == 0:
                        wge(t, "hq", OS_TICK[sh][u // 4])
                    if g >= 1:
                        wge(t, "ov", ovO2(g - 1))
                    pst = pairst(g % 2)
                    for n in range(2):
                        last = t.matmul(
                            psF2[:, n * 512:(n + 1) * 512], c_p2[:],
                            bass.AP(pst.tensor,
                                    pst.offset + sh * HW2 + n * 512,
                                    [pst.ap[0], [1, 512]]),
                            start=True, stop=True)
                    inc(last, "og")
                for st8 in range(8):
                    gst = (sh * 8 + r) * 8 + st8
                    wge(t, "ov", ovO2(sh * 64 + r * 8 + 7))
                    if gst >= 2:
                        p = gst - 2
                        wge(t, "ov", ovAdd(p // 64, (p % 64) // 8, p % 8))
                    wge(t, "ow0", OW_TICK[sh][r * 8 + 6])
                    wge(t, "ow1", OW_TICK[sh][r * 8 + 7])
                    ps = psF1[:, (st8 % 2) * 1024:(st8 % 2) * 1024 + 1024]
                    for ui in range(8):
                        u = r * 8 + ui
                        for hf in range(2):
                            last = t.matmul(
                                ps[:, hf * 512:(hf + 1) * 512],
                                o2sl(ui)[:, st8 * 128:(st8 + 1) * 128],
                                wosl(u % 8)[:, hf * 512:(hf + 1) * 512],
                                start=(ui == 0), stop=(ui == 7))
                    inc(last, "og")

    ASPC = 32 if stop == "G" else 16

    def epi_v(v, i):
        sh, s8 = i // 8, i % 8
        acc = attn_acc[:, s8 * D:(s8 + 1) * D]
        b = i % 2
        wge(v, "hr", HQ2_SRC[i])
        inc(v.tensor_tensor(acc, acc, srcrl[b][:], op=OP.add), "e2v")
        wge(v, "e2v", 5 * i + 1)
        if i >= 2:
            wge(v, "e2s", 3 * (i - 2) + 2)
        inc(v.tensor_reduce(epO["sum"][:, b:b + 1], acc, axis=AX.X,
                            op=OP.add), "e2v")
        wge(v, "e2s", 3 * i + 2)
        inc(v.scalar_tensor_tensor(epO["var"][:, b:b + 1],
                                   epO["ssq"][:, b:b + 1], 1.0 / D,
                                   epO["m2"][:, b:b + 1],
                                   op0=OP.mult, op1=OP.subtract), "e2v")
        wge(v, "e2s", 3 * i + 3)
        if i >= 8:
            wge(v, "e2t", i - 7)
            wge(v, "asp", (i - 7) * ASPC)
        fmn = inc(v.tensor_scalar(sjunk[:], acc, epO["nm"][:, b:b + 1],
                                  epO["r"][:, b:b + 1],
                                  op0=OP.add, op1=OP.mult), "vf")
        wge(v, "vf", fmn)
        fg = inc(v.tensor_tensor(sjunk[:], sjunk[:], g1b[:], op=OP.mult), "vf")
        wge(v, "vf", fg)
        last = v.tensor_tensor(aln_tm[i % 8][:], sjunk[:], b1b[:], op=OP.add)
        inc(last, "e2v")
        inc(v.tensor_copy(epO["lv"][:, b:b + 1], epO["r"][:, b:b + 1]), "e2v")

    @block.vector
    def _(v):
        wge(v, "hq2", 32)
        for sh in range(2):
            for r in range(8):
                for ui in range(8):
                    u = r * 8 + ui
                    g = sh * 64 + u
                    wge(v, "og", ogGb(g))
                    wge(v, "hq", OS_TICK[sh][u // 4])
                    if g >= 8:
                        p = g - 8
                        wge(v, "og", ogSt(p // 64, (p % 64) // 8, 7))
                    inc(v.tensor_tensor(o2sl(ui)[:, :], ostack[:, :],
                                        psF2[:], op=OP.mult), "ov")
                for st8 in range(8):
                    wge(v, "og", ogSt(sh, r, st8))
                    if sh == 1 and r == 0:
                        if st8 == 0:
                            wge(v, "asp", 8 * ASPC)
                        wge(v, "e2s", 3 * st8 + 1)
                    ps = psF1[:, (st8 % 2) * 1024:(st8 % 2) * 1024 + 1024]
                    acc = attn_acc[:, st8 * D:(st8 + 1) * D]
                    if r == 0:
                        inc(v.tensor_scalar(acc, ps, 1.0, None, op0=OP.mult), "ov")
                    else:
                        inc(v.tensor_tensor(acc, acc, ps, op=OP.add), "ov")
            for s8 in range(8):
                epi_v(v, sh * 8 + s8)

    def epi_aevac(s, j):
        wge(s, "e2t", j + 1)
        half = (j % 2) * 1024
        dst = bass.AP(act_a[:].tensor, act_a[:].offset + j * 128,
                      [act_a[:].ap[0], [S, 8], [1, 128]])
        inc(s.copy(dst, psE[:, half:half + 1024].rearrange(
            "p (j c) -> p j c", j=8)), "e3s")

    @block.scalar
    def _(s):
        for i in range(16):
            b = i % 2
            wge(s, "e2v", 5 * i + 1)
            inc(s.activation(sjunk[:], attn_acc[:, (i % 8) * D:(i % 8 + 1) * D],
                             AF.Square, accum_out=epO["ssq"][:, b:b + 1]), "e2s")
            wge(s, "e2v", 5 * i + 2)
            s.activation(epO["m2"][:, b:b + 1], epO["sum"][:, b:b + 1],
                         AF.Square, scale=1.0 / D)
            inc(s.activation(epO["nm"][:, b:b + 1], epO["sum"][:, b:b + 1],
                             AF.Copy, scale=-1.0 / D), "e2s")
            wge(s, "e2v", 5 * i + 3)
            last = s.activation(epO["lv"][:, b:b + 1], epO["var"][:, b:b + 1],
                                AF.Ln, bias=c_eps[:])
            wge(s, "sf", inc(last, "sf"))
            inc(s.activation(epO["r"][:, b:b + 1], epO["lv"][:, b:b + 1],
                             AF.Exp, scale=-0.5), "e2s")
            if i >= 8:
                epi_aevac(s, i - 8)
        for j in range(8, 16):
            epi_aevac(s, j)

    @block.tensor
    def _(t):
        wge(t, "ov", ovO2(127))
        for i in range(16):
            wge(t, "e2v", 5 * i + 4)
            if i >= 2:
                wge(t, "e3s", i - 1)
            half = (i % 2) * 1024
            for j in range(8):
                last = t.transpose(psE[:, half + j * 128: half + (j + 1) * 128],
                                   aln_tm[i % 8][:, j * 128:(j + 1) * 128], c_id[:])
            inc(last, "e2t")

    ALN_T_DONE = cnt["e3s"]

    STOPG = (stop == "G")
    if STOPG:
        @block.sync
        def _(sync):
            for nm in list(semn):
                if nm in ("t", "v", "s", "eps", "vf", "sf", "dts", "da", "scv",
                          "zt", "szs", "gy", "ot", "epv", "ep2", "gt", "gs",
                          "gl", "gv2", "og", "ov", "e2v", "e2s", "e2t", "e3s",
                          "ht", "hs", "hv", "f2v", "f2s"):
                    continue
                wge(sync, nm, cnt[nm])
            wge(sync, "e3s", cnt["e3s"])
        esO.close()
        esH1.close() if False else None
        esE.close()
        es.close()
        return nc, tap_handles

    esO.close()

    w1sl = lambda m: xc_a[:, m * 1024:(m + 1) * 1024]
    w2sl = lambda m: xc_a[:, 8192 + m * 1024: 8192 + (m + 1) * 1024]
    hidsl = lambda m: xc_a[:, 16384 + m * 512: 16384 + (m + 1) * 512]
    esH = ExitStack()
    b2b = esH.enter_context(nc.sbuf_tensor("b2b", [128, D], F32))
    g2b = esH.enter_context(nc.sbuf_tensor("g2b", [128, D], F32))
    b2gb = esH.enter_context(nc.sbuf_tensor("b2gb", [128, D], F32))
    arl = [esH.enter_context(nc.sbuf_tensor(f"arlH{i}", [128, D], BF16))
           for i in range(2)]
    x1rl = [esH.enter_context(nc.sbuf_tensor(f"x1rlH{i}", [128, D], BF16))
            for i in range(2)]
    outb = [esH.enter_context(nc.sbuf_tensor(f"outbH{i}", [128, D], F32))
            for i in range(2)]
    sjunk2 = esH.enter_context(nc.sbuf_tensor("sjunk2", [128, D], F32))
    epH = {nm: esH.enter_context(nc.sbuf_tensor("epH_" + nm, [128, 2], F32))
           for nm in ["sum", "nm", "m2", "ssq", "var", "lv", "r",
                      "sum2", "nm2", "m22", "ssq2", "var2", "lv2", "r2"]}

    htW1 = lambda sh, r, sc, m: ((sh * 4 + r) * 2 + sc) * 12 + m + 1
    htW2 = lambda sh, r, sc, stl: ((sh * 4 + r) * 2 + sc) * 12 + 8 + stl + 1
    hsR = lambda sh, r, sc, m: ((sh * 4 + r) * 2 + sc) * 8 + m + 1
    hvA = lambda sh, r, sc, stl: ((sh * 4 + r) * 2 + sc) * 4 + stl + 1
    HW_TICK = [0] * 8

    @block.gpsimd
    def _(gp):
        for sh in range(2):
            for r in range(4):
                ri = sh * 4 + r
                if ri == 0:
                    wge(gp, "og", ogSt(1, 7, 7))
                else:
                    wge(gp, "ht", htW2(*( (ri - 1) // 4, (ri - 1) % 4, 1, 3)))
                for m in range(8):
                    dma(gp, "hw0", w1sl(m)[:, :], d_w1b[r * 8 + m])
                HW_TICK[ri] = cnt["hw0"]


    HQ2_MARK = {}
    @block.sync
    def _(sync):
        wge(sync, "e2t", 16)
        dma(sync, "hq2", b2b[:], repl_ap(d_b2r, D))
        HQ2_MARK["b2b"] = cnt["hq2"]
        dma(sync, "hq2", g2b[:], repl_ap(d_lng2, D))
        dma(sync, "hq2", b2gb[:], repl_ap(d_lnb2, D))
        HQ2_MARK["all"] = cnt["hq2"]

    @block.tensor
    def _(t):
        wge(t, "e3s", 16)
        for sh in range(2):
            for r in range(4):
                ri = sh * 4 + r
                wge(t, "hw0", 16 * 8 * (ri + 1))
                wge(t, "hw1", 16 * 8 * (ri + 1))
                for sc in range(2):
                    for m in range(8):
                        gm = ((sh * 4 + r) * 2 + sc) * 8 + m
                        if gm >= 2:
                            wge(t, "hs", gm - 1)
                        cs = sh * HW2 + sc * 512
                        ps = psF2[:, (m % 2) * 512:(m % 2) * 512 + 512]
                        for k in range(8):
                            last = t.matmul(ps, w1sl(m)[:, k * 128:(k + 1) * 128],
                                            ln0T(k)[:, cs:cs + 512],
                                            start=(k == 0), stop=(k == 7))
                        inc(last, "ht")
                    for stl in range(4):
                        gst = ((sh * 4 + r) * 2 + sc) * 4 + stl
                        wge(t, "hs", hsR(sh, r, sc, 7))
                        if gst >= 2:
                            wge(t, "hv", gst - 1)
                        ps = psF1[:, (stl % 2) * 1024:(stl % 2) * 1024 + 1024]
                        for m in range(8):
                            for hf in range(2):
                                last = t.matmul(ps[:, hf * 512:(hf + 1) * 512],
                                                hidsl(m)[:, stl * 128:(stl + 1) * 128],
                                                w2sl(m)[:, hf * 512:(hf + 1) * 512],
                                                start=(m == 0), stop=(m == 7))
                        inc(last, "ht")


    AR_TICK = [0] * 16
    X1_TICK = [0] * 16
    @block.sync
    def _(sync):
        wge(sync, "asp", cnt["asp"])
        for i in range(16):
            sh, s8 = i // 8, i % 8
            if i >= 2:
                wge(sync, "f2v", 6 * (i - 2) + 1)
            AR_TICK[i] = dma(sync, "ar", arl[i % 2][:],
                             d_asp[i * 128:(i + 1) * 128, :])
            if i >= 2:
                wge(sync, "f2v", 6 * (i - 2) + 6)
            X1_TICK[i] = dma(sync, "xr2", x1rl[i % 2][:],
                             d_x1sp[i * 128:(i + 1) * 128, :])
            wge(sync, "f2v", 6 * i + 6)
            dma(sync, "outq0" if i % 2 == 0 else "outq1",
                d_out[i * 128:(i + 1) * 128, :], outb[i % 2][:])

    def epi_f2s(s, i):
        b = i % 2
        slot = attn_acc[:, (i % 8) * D:(i % 8 + 1) * D]
        wge(s, "f2v", 6 * i + 1)
        inc(s.activation(sjunk[:], slot, AF.Square,
                         accum_out=epH["ssq"][:, b:b + 1]), "f2s")
        wge(s, "f2v", 6 * i + 2)
        s.activation(epH["m2"][:, b:b + 1], epH["sum"][:, b:b + 1],
                     AF.Square, scale=1.0 / D)
        inc(s.activation(epH["nm"][:, b:b + 1], epH["sum"][:, b:b + 1],
                         AF.Copy, scale=-1.0 / D), "f2s")
        wge(s, "f2v", 6 * i + 3)
        last = s.activation(epH["lv"][:, b:b + 1], epH["var"][:, b:b + 1],
                            AF.Ln, bias=c_eps[:])
        wge(s, "sf", inc(last, "sf"))
        inc(s.activation(epH["r"][:, b:b + 1], epH["lv"][:, b:b + 1],
                         AF.Exp, scale=-0.5), "f2s")
        wge(s, "f2v", 6 * i + 4)
        inc(s.activation(sjunk2[:], sjunk[:], AF.Square,
                         accum_out=epH["ssq2"][:, b:b + 1]), "f2s")
        s.activation(epH["m22"][:, b:b + 1], epH["sum2"][:, b:b + 1],
                     AF.Square, scale=1.0 / D)
        inc(s.activation(epH["nm2"][:, b:b + 1], epH["sum2"][:, b:b + 1],
                         AF.Copy, scale=-1.0 / D), "f2s")
        wge(s, "f2v", 6 * i + 5)
        last = s.activation(epH["lv2"][:, b:b + 1], epH["var2"][:, b:b + 1],
                            AF.Ln, bias=c_eps[:])
        wge(s, "sf", inc(last, "sf"))
        inc(s.activation(epH["r2"][:, b:b + 1], epH["lv2"][:, b:b + 1],
                         AF.Exp, scale=-0.5), "f2s")

    @block.scalar
    def _(s):
        wge(s, "pc", PC_ALL)
        for sh in range(2):
            for r in range(4):
                ri = sh * 4 + r
                if ri == 0:
                    wge(s, "og", ogSt(1, 7, 7))
                else:
                    wge(s, "ht", htW2(*( (ri - 1) // 4, (ri - 1) % 4, 1, 3)))
                for m in range(8):
                    dma(s, "hw1", w2sl(m)[:, :], d_w2b[r * 8 + m])
                for sc in range(2):
                    for m in range(8):
                        wge(s, "ht", htW1(sh, r, sc, m))
                        if not (sh == 0 and r == 0 and sc == 0):
                            prev = ((sh * 4 + r) * 2 + sc) - 1
                            wge(s, "ht", prev * 12 + 8 + 4)
                        ps = psF2[:, (m % 2) * 512:(m % 2) * 512 + 512]
                        inc(s.activation(hidsl(m)[:, 0:512], ps, AF.Relu,
                                         bias=c_b1[:, r * 8 + m: r * 8 + m + 1]), "hs")
            for s8 in range(8):
                epi_f2s(s, sh * 8 + s8)

    def epi_f2v(v, i):
        sh, s8 = i // 8, i % 8
        b = i % 2
        slot = attn_acc[:, s8 * D:(s8 + 1) * D]
        wge(v, "ar", AR_TICK[i])
        inc(v.tensor_tensor(slot, slot, arl[b][:], op=OP.add), "f2v")
        wge(v, "f2v", 6 * i + 1)
        if i >= 2:
            wge(v, "f2s", 6 * (i - 2) + 2)
        inc(v.tensor_reduce(epH["sum"][:, b:b + 1], slot, axis=AX.X,
                            op=OP.add), "f2v")
        wge(v, "f2s", 6 * i + 2)
        inc(v.scalar_tensor_tensor(epH["var"][:, b:b + 1],
                                   epH["ssq"][:, b:b + 1], 1.0 / D,
                                   epH["m2"][:, b:b + 1],
                                   op0=OP.mult, op1=OP.subtract), "f2v")
        wge(v, "f2s", 6 * i + 3)
        fmn = inc(v.tensor_scalar(sjunk[:], slot, epH["nm"][:, b:b + 1],
                                  epH["r"][:, b:b + 1],
                                  op0=OP.add, op1=OP.mult), "vf")
        wge(v, "vf", fmn)
        fg = inc(v.tensor_tensor(sjunk[:], sjunk[:], g2b[:], op=OP.mult), "vf")
        wge(v, "vf", fg)
        fb = inc(v.tensor_tensor(sjunk[:], sjunk[:], b2gb[:], op=OP.add), "vf")
        wge(v, "vf", fb)
        inc(v.tensor_reduce(epH["sum2"][:, b:b + 1], sjunk[:], axis=AX.X,
                            op=OP.add), "f2v")
        wge(v, "f2s", 6 * i + 5)
        inc(v.scalar_tensor_tensor(epH["var2"][:, b:b + 1],
                                   epH["ssq2"][:, b:b + 1], 1.0 / D,
                                   epH["m22"][:, b:b + 1],
                                   op0=OP.mult, op1=OP.subtract), "f2v")
        wge(v, "f2s", 6 * i + 6)
        if i >= 2:
            wge(v, "outq0" if i % 2 == 0 else "outq1",
                16 * (i // 2))
        fo = inc(v.tensor_scalar(outb[b][:], sjunk[:], epH["nm2"][:, b:b + 1],
                                 epH["r2"][:, b:b + 1],
                                 op0=OP.add, op1=OP.mult), "vf")
        wge(v, "vf", fo)
        wge(v, "xr2", X1_TICK[i])
        inc(v.tensor_tensor(outb[b][:], outb[b][:], x1rl[b][:], op=OP.add),
            "f2v")

    E2S_END = cnt["e2s"]
    @block.vector
    def _(v):
        wge(v, "asp", cnt["asp"])
        wge(v, "hq2", HQ2_MARK["all"])
        wge(v, "e2s", E2S_END)
        for sh in range(2):
            for r in range(4):
                for sc in range(2):
                    for stl in range(4):
                        gst = ((sh * 4 + r) * 2 + sc) * 4 + stl
                        wge(v, "ht", htW2(sh, r, sc, stl))
                        ps = psF1[:, (stl % 2) * 1024:(stl % 2) * 1024 + 1024]
                        slot = attn_acc[:, (sc * 4 + stl) * D:(sc * 4 + stl + 1) * D]
                        if r == 0:
                            if sh == 1:
                                wge(v, "f2v", 6 * (sc * 4 + stl) + 1)
                                wge(v, "f2s", 6 * (sc * 4 + stl) + 1)
                            inc(v.tensor_tensor(slot, ps, b2b[:], op=OP.add), "hv")
                        else:
                            inc(v.tensor_tensor(slot, slot, ps, op=OP.add), "hv")
            for s8 in range(8):
                epi_f2v(v, sh * 8 + s8)



    @block.sync
    def _(sync):
        for nm in ["qc", "qga", "qcx", "ax0", "ax1", "bw0", "bw1", "gb",
                   "wz0", "wz1", "wo0", "wo1", "wo2", "xps", "sx0", "sx1",
                   "ssr", "xr", "eq", "fw0", "fw1", "hq", "oq0", "oq1", "po0", "po1",
                   "ow0", "ow1", "hw0", "hw1", "hq2", "ar", "xr2", "asp",
                   "outq0", "outq1", "pc", "hr"]:
            wge(sync, nm, cnt[nm])
        wge(sync, "f2v", cnt["f2v"])

    esH.close()
    esE.close()
    es.close()
    return nc, tap_handles


def kernel(**inputs):
    aux = _host_prep(inputs)
    nc, tap_handles = _build(tuple(_TAPS), _STOP)
    in_maps = []
    for c in range(8):
        m = dict(aux)
        m["xin"] = np.ascontiguousarray(np.asarray(inputs["L0_attn_x"][c], np.float32))
        in_maps.append(m)
    res = run_bass_kernel_spmd(nc, in_maps, core_ids=list(range(8)))
    for name in _TAPS:
        TAP_RESULTS[name] = [r["tap_" + name] for r in res.results]
    return np.stack([np.asarray(r["out"], np.float32) for r in res.results], axis=0)
